# version 12
# baseline (speedup 1.0000x reference)
import numpy as np
import ml_dtypes
import concourse.bass as bass
import concourse.mybir as mybir
from concourse.bass_utils import run_bass_kernel_spmd
from contextlib import ExitStack

F32 = mybir.dt.float32
BF16 = mybir.dt.bfloat16
I32 = mybir.dt.int32
AF = mybir.ActivationFunctionType
ALU = mybir.AluOpType

B, T, D = 4, 2048, 4096
H, HKV, DH, G = 32, 8, 128, 4
NC32 = 32
OWN = 1024
EPS = 1e-5
SCL = float(1.0 / np.sqrt(128.0))
EXP_B = -12.0
MAGIC = float(3 * 2 ** 22)
INV2PI = float(1.0 / (2 * np.pi))
NEG2PI = float(-2.0 * np.pi)
NCHUNK = 20
NCONST = 9


def _fit_poly():
    dom = [0.0, (np.pi * 1.001) ** 2]
    u = np.linspace(dom[0], dom[1], 20001)
    x = np.sqrt(u)
    fs = np.where(x == 0, 1.0, np.sin(x) / np.where(x == 0, 1, x))
    cs = np.polynomial.chebyshev.Chebyshev.fit(u, fs, 5, domain=dom) \
        .convert(kind=np.polynomial.Polynomial).coef
    cc = np.polynomial.chebyshev.Chebyshev.fit(u, np.cos(x), 6, domain=dom) \
        .convert(kind=np.polynomial.Polynomial).coef
    return cs, cc


_PS, _PC = _fit_poly()
SIN_LEAD = float(_PS[-1]); SIN_D = [float(c) for c in (_PS / _PS[-1])[:-1]]
COS_LEAD = float(_PC[-1]); COS_D = [float(c) for c in (_PC / _PC[-1])[:-1]]


def csched(j):
    if j == 0:
        a = [(0, c, c) for c in range(4)]
        b = [(1, c, None) for c in [8, 9, 10, 11, 12, 13, 14, 15, 0, 1, 2, 3]] \
            + [(1, 4 + i, i) for i in range(4)]
    else:
        a = [(0, c, None) for c in [8, 9, 10, 11]] + [(0, i, i) for i in range(4)]
        b = [(1, c, None) for c in [8, 9, 10, 11, 0, 1, 2, 3]] \
            + [(1, 4 + i, i) for i in range(4)]
    return a + b


def build_sched():
    sched = []
    uG = uR = uV = 0
    slab_g0 = []
    slab_r0 = []
    for s in range(4):
        slab_g0.append(uG)
        slab_r0.append(uR)
        for m in range(16):
            if m < 8:
                sched.append((s, 'K', m, uG, uR)); uR += 1
            else:
                sched.append((s, 'V', m - 8, uG, uV)); uV += 1
            uG += 1
        if s < 2:
            for m in range(32):
                sched.append((s, 'Q', m, uG, uR)); uR += 1
                uG += 1
    return sched, slab_g0, slab_r0, uG, uR, uV


SCHED, SLAB_G0, SLAB_R0, NG, NR, NV = build_sched()
SCHED_BY_SLAB = [[e for e in SCHED if e[0] == s] for s in range(4)]
CONS = {}
for (s, kind, m, uG, ur) in SCHED:
    CONS[uG] = ('dVE', ur + 1) if kind == 'V' else ('dKE', ur + 1)


def build_program():
    nc = bass.Bass("TRN2", target_bir_lowering=False, debug=False)

    xf = nc.dram_tensor("xf", [D, T], F32, kind="ExternalInput").ap()
    io = nc.dram_tensor("io", [1, T], F32, kind="ExternalInput").ap()
    i64 = nc.dram_tensor("i64", [1, 64], F32, kind="ExternalInput").ap()
    sel = nc.dram_tensor("sel", [1, 1], I32, kind="ExternalInput").ap()
    wq = nc.dram_tensor("wq", [32, 128, 32, 128], F32, kind="ExternalInput").ap()
    wk = nc.dram_tensor("wk", [8, 128, 32, 128], F32, kind="ExternalInput").ap()
    wv = nc.dram_tensor("wv", [8, 128, 32, 128], F32, kind="ExternalInput").ap()
    wo = nc.dram_tensor("wo", [32, 128, 32, 128], F32, kind="ExternalInput").ap()
    wf = nc.dram_tensor("wf", [32, 128, 32, 128], F32, kind="ExternalInput").ap()
    wa = nc.dram_tensor("wa", [2, 128, 32, 128], F32, kind="ExternalInput").ap()
    wb = nc.dram_tensor("wb", [32, 128, 2, 128], F32, kind="ExternalInput").ap()
    bi = nc.dram_tensor("bi", [128, 32], F32, kind="ExternalInput").ap()
    cf = nc.dram_tensor("cf", [16, 1], F32, kind="ExternalInput").ap()
    scl = nc.dram_tensor("scl", [16, 1], F32, kind="ExternalInput").ap()
    e16 = nc.dram_tensor("e16", [2, 16, 128], F32, kind="ExternalInput").ap()
    alw = nc.dram_tensor("alw", [128, 128], BF16, kind="ExternalInput").ap()
    bsh = nc.dram_tensor("bsh", [128, 896], BF16, kind="ExternalInput").ap()
    idn = nc.dram_tensor("idn", [128, 128], BF16, kind="ExternalInput").ap()
    out = nc.dram_tensor("out", [D, OWN], F32, kind="ExternalOutput").ap()
    x2scr = nc.dram_tensor("x2scr", [D, OWN], F32, kind="Internal").ap()

    ctx = ExitStack()
    sb = lambda n, s, d: ctx.enter_context(nc.sbuf_tensor(n, s, d))
    pst = lambda n, s, d: ctx.enter_context(nc.psum_tensor(n, s, d))
    sm = lambda n: ctx.enter_context(nc.semaphore(n))

    QA = sb("QA", [128, H * OWN], BF16)
    SLB = sb("SLB", [128, NC32 * 512], BF16)
    KT = sb("KT", [128, HKV * T], BF16)
    VSB = sb("VSB", [128, 16 * 1024], BF16)
    WB = sb("WB", [128, 4096], BF16)
    BT = sb("BT", [128, 512], BF16)
    PFG = sb("PFG", [128, 4096], F32)
    PSM = sb("PSM", [128, 2048], BF16)
    SRF = sb("SRF", [1, 3328], F32)
    alw_sb = sb("alw_sb", [128, 128], BF16)
    bsh_sb = sb("bsh_sb", [128, 896], BF16)
    idn_sb = sb("idn_sb", [128, 128], BF16)
    e16_sb = sb("e16_sb", [16, 256], F32)
    bias_sb = sb("bias_sb", [128, 32], F32)
    cf_sb = sb("cf_sb", [16, 1], F32)
    sc_sb = sb("sc_sb", [16, 1], F32)
    wvec = sb("wvec", [16, 1], F32)
    wexp = sb("wexp", [128, 2], F32)
    ones_bf = sb("ones_bf", [128, 1], BF16)
    ones128 = sb("ones128", [128, 128], BF16)
    sel_sb = sb("sel_sb", [1, 1], I32)
    i64_sb = sb("i64_sb", [1, 64], F32)

    def srf(lo, n):
        return SRF[0:1, lo: lo + n]
    IOF, MEAN, RSTD, SVAR, SSD, VTMP, ONES1, INVF = \
        0, 512, 1024, 1536, 2048, 2560, 3072, 3200

    def pfg(lo, n):
        return PFG[:, lo: lo + n]

    def psm(lo, n):
        return PSM[:, lo: lo + n]

    def slb(c, lo, n):
        return SLB[:, c * 512 + lo: c * 512 + lo + n]

    def qa(h, lo, n):
        return QA[:, h * OWN + lo: h * OWN + lo + n]

    P1 = pst("P1", [128, 1024], F32)
    P2 = pst("P2", [128, 1024], F32)
    P3 = pst("P3", [128, 1024], F32)
    P5 = pst("P5", [128, 2048], BF16)

    s_ld = sm("s_ld"); s_IO = sm("s_IO"); s_TB = sm("s_TB")
    s_RH0 = sm("s_RH0"); s_RH1 = sm("s_RH1")
    s_X0 = sm("s_X0"); s_X1 = sm("s_X1")
    s_SC0 = sm("s_SC0"); s_SC1 = sm("s_SC1")
    s_X20 = sm("s_X20"); s_X21 = sm("s_X21")
    s_OUT0 = sm("s_OUT0"); s_OUT1 = sm("s_OUT1")
    s_CP0 = sm("s_CP0"); s_CP1 = sm("s_CP1")
    g_X0 = sm("g_X0"); g_X1 = sm("g_X1"); g_A = sm("g_A")
    g_W0 = sm("g_W0"); g_W1 = sm("g_W1")
    g_Wo0 = sm("g_Wo0"); g_Wo1 = sm("g_Wo1")
    g_Wf0 = sm("g_Wf0"); g_Wf1 = sm("g_Wf1")
    g_BT0 = sm("g_BT0"); g_BT1 = sm("g_BT1")
    g_X20 = sm("g_X20"); g_X21 = sm("g_X21")
    pW = sm("pW"); pTB = sm("pTB"); pS = sm("pS"); pBC = sm("pBC")
    pWH = sm("pWH"); pVT = sm("pVT")
    pSc = sm("pSc"); pAV = sm("pAV"); pLB = sm("pLB")
    pD = sm("pD"); pS2 = sm("pS2"); pBC2 = sm("pBC2"); pXA = sm("pXA")
    pF = sm("pF")
    aTB = sm("aTB"); aW = sm("aW"); aSQ = sm("aSQ"); aST = sm("aST")
    aKE = sm("aKE"); aVE = sm("aVE"); aVT = sm("aVT")
    aCE = sm("aCE"); dKE = sm("dKE"); dVE = sm("dVE")
    aE = sm("aE"); aOC = sm("aOC"); aCP = sm("aCP"); aST2 = sm("aST2")
    aF = sm("aF")
    dCN = sm("dCN"); dTB = sm("dTB"); dST = sm("dST"); dRS = sm("dRS")
    dAP = sm("dAP"); dRP = sm("dRP")
    dREC = sm("dREC"); dNR = sm("dNR")
    dX2 = sm("dX2"); dST2 = sm("dST2"); dRS2 = sm("dRS2")
    dAPE = sm("dAPE"); dXA = sm("dXA"); dOUT = sm("dOUT")

    def W(e, s, v):
        if v > 0:
            e.wait_ge(s, v)

    def span_ends(j):
        nA = 4 if j == 0 else 8
        return [(0, nA - 1), (1, NCHUNK - 1)]

    block = ctx.enter_context(nc.Block())

    zf = np.float32(0.0)
    onef = np.float32(1.0)

    @block.sync
    def _(s):
        s.dma_start(i64_sb[:], i64).then_inc(s_ld, 16)
        s.dma_start(sel_sb[:], sel).then_inc(s_ld, 16)
        s.dma_start(alw_sb[:], alw).then_inc(s_ld, 16)
        s.dma_start(bsh_sb[:], bsh).then_inc(s_ld, 16)
        s.dma_start(idn_sb[:], idn).then_inc(s_ld, 16)
        s.dma_start(e16_sb[:].rearrange("e (k c) -> e k c", k=2),
                    e16.rearrange("k e c -> e k c")).then_inc(s_ld, 16)
        s.dma_start(bias_sb[:], bi).then_inc(s_ld, 16)
        s.dma_start(cf_sb[:], cf).then_inc(s_ld, 16)
        s.dma_start(sc_sb[:], scl).then_inc(s_ld, 16)
        for s_i in range(4):
            W(s, pTB, s_i)
            s.dma_start(srf(IOF, 512),
                        io[0:1, s_i * 512:(s_i + 1) * 512]).then_inc(s_IO, 16)
            s.wait_ge(dTB, s_i + 1)
            s.dma_start(PFG[64:128, 0:512], PFG[0:64, 0:512]).then_inc(s_TB, 16)
            s.dma_start(PFG[64:128, 512:1024],
                        PFG[0:64, 1024:1536]).then_inc(s_TB, 16)
            r0 = SLAB_R0[s_i]
            nrope = 40 if s_i < 2 else 8
            for rl in range(nrope):
                r = r0 + rl
                s.wait_ge(dKE, r + 1)
                W(s, dRP, r - 1)
                stg = pfg(2048 + (r % 2) * 512, 512)
                rh = pfg(3072 + (r % 2) * 512, 512)
                rsem = s_RH0 if r % 2 == 0 else s_RH1
                s.dma_start(rh[0:64, :], stg[64:128, :]).then_inc(rsem, 16)
                s.dma_start(rh[64:128, :], stg[0:64, :]).then_inc(rsem, 16)
        s.wait_ge(dNR, 64)
        for m in range(32):
            W(s, dX2, m - 1)
            s.dma_start(pfg(2048 + (m % 2) * 1024, 1024),
                        xf[m * 128:(m + 1) * 128, 0:OWN]) \
                .then_inc(s_X0 if m % 2 == 0 else s_X1, 16)
            s.wait_ge(dX2, m + 1)
            W(s, s_SC0 if m % 2 == 0 else s_SC1, 16 * (m // 2))
            s.dma_start(x2scr[m * 128:(m + 1) * 128, :],
                        pfg((m % 2) * 1024, 1024)) \
                .then_inc(s_SC0 if m % 2 == 0 else s_SC1, 16)
        s.wait_ge(dX2, 32)
        for m in range(32):
            s.wait_ge(s_SC0 if m % 2 == 0 else s_SC1, 16 * (m // 2 + 1))
            W(s, s_OUT0 if m % 2 == 0 else s_OUT1, 16 * (m // 2))
            s.dma_start(out[m * 128:(m + 1) * 128, :],
                        x2scr[m * 128:(m + 1) * 128, :]) \
                .then_inc(s_CP0 if m % 2 == 0 else s_CP1, 16)
        s.wait_ge(s_OUT0, 16 * 16)
        s.wait_ge(s_OUT1, 16 * 16)

    @block.gpsimd
    def _(g):
        xfr = xf.rearrange("(c p) t -> p c t", p=128)
        x2r = x2scr.rearrange("(c p) t -> p c t", p=128)
        for s_i in range(4):
            for c in range(NC32):
                if s_i >= 1 and c % 16 == 0:
                    W(g, pWH, 2 * SLAB_G0[s_i] - 1 + c // 16)
                g.dma_start(slb(c, 0, 512),
                            xfr[:, c, s_i * 512:(s_i + 1) * 512]) \
                    .then_inc(g_X0 if c < 16 else g_X1, 16)
            for (_s, kind, m, uG, ur) in SCHED_BY_SLAB[s_i]:
                wsrc = {'K': wk, 'V': wv, 'Q': wq}[kind]
                for h in range(2):
                    w_ = 2 * uG + h
                    W(g, pWH, w_ - 1)
                    g.dma_start(
                        WB[:, h * 2048: h * 2048 + 2048]
                        .rearrange("p (k c) -> p k c", k=16),
                        wsrc[m, :, 16 * h:16 * h + 16, :]) \
                        .then_inc(g_W0 if h == 0 else g_W1, 16)
        g.wait_ge(pWH, 2 * NG)
        for m in range(32):
            for h in range(2):
                w_ = 2 * m + h
                W(g, pD, w_ - 1)
                g.dma_start(
                    WB[:, h * 2048: h * 2048 + 2048]
                    .rearrange("p (k c) -> p k c", k=16),
                    wo[m, :, 16 * h:16 * h + 16, :]) \
                    .then_inc(g_Wo0 if h == 0 else g_Wo1, 16)
            if 2 <= m < 10:
                c = m - 2
                W(g, s_SC0 if c % 2 == 0 else s_SC1, 16 * (c // 2 + 1))
                W(g, g_X20 if c % 2 == 0 else g_X21, 16 * (c // 2))
                g.dma_start(SLB[:, 6144 + (c % 8) * 1024:
                                6144 + (c % 8) * 1024 + 1024],
                            x2r[:, c, :]) \
                    .then_inc(g_X20 if c % 2 == 0 else g_X21, 16)
        g.wait_ge(pD, 64)
        for h in range(2):
            g.dma_start(
                WB[:, h * 2048: h * 2048 + 2048]
                .rearrange("p (k c) -> p k c", k=16),
                wa[0, :, 16 * h:16 * h + 16, :]).then_inc(g_A, 16)
        for c in range(8, NC32):
            W(g, dAPE, c - 7)
            W(g, s_SC0 if c % 2 == 0 else s_SC1, 16 * (c // 2 + 1))
            W(g, g_X20 if c % 2 == 0 else g_X21, 16 * (c // 2))
            g.dma_start(SLB[:, 6144 + (c % 8) * 1024:
                            6144 + (c % 8) * 1024 + 1024],
                        x2r[:, c, :]) \
                .then_inc(g_X20 if c % 2 == 0 else g_X21, 16)
        for h in range(2):
            if h == 0:
                g.wait_ge(pXA, 2)
            g.dma_start(
                WB[:, h * 2048: h * 2048 + 2048]
                .rearrange("p (k c) -> p k c", k=16),
                wa[1, :, 16 * h:16 * h + 16, :]).then_inc(g_A, 16)
        g.wait_ge(pXA, 4)
        for mm_ in range(34):
            if mm_ < 32:
                m = mm_
                for h in range(2):
                    w_ = 2 * m + h
                    W(g, pF, w_ - 1)
                    g.dma_start(
                        WB[:, h * 2048: h * 2048 + 2048]
                        .rearrange("p (k c) -> p k c", k=16),
                        wf[m, :, 16 * h:16 * h + 16, :]) \
                        .then_inc(g_Wf0 if h == 0 else g_Wf1, 16)
                W(g, pF, 2 * m - 2)
                g.dma_start(BT[:, (m % 2) * 256:(m % 2) * 256 + 256]
                            .rearrange("p (k c) -> p k c", k=2),
                            wb[m]).then_inc(g_BT0 if m % 2 == 0 else g_BT1, 16)
            if mm_ >= 2:
                m = mm_ - 2
                g.wait_ge(aF, m + 1)
                g.wait_ge(s_CP0 if m % 2 == 0 else s_CP1, 16 * (m // 2 + 1))
                g.dma_start(out[m * 128:(m + 1) * 128, :],
                            pfg((m % 2) * 1024, 1024),
                            accum_op=ALU.add) \
                    .then_inc(s_OUT0 if m % 2 == 0 else s_OUT1, 16)
        g.wait_ge(s_OUT0, 16 * 16)
        g.wait_ge(s_OUT1, 16 * 16)

    @block.tensor
    def _(t):
        t.wait_ge(dCN, 3)
        for k2 in range(2):
            t.matmul(P3[:, k2:k2 + 1], e16_sb[:, k2 * 128:(k2 + 1) * 128],
                     wvec[:], start=True, stop=True).then_inc(pW, 1)
        t.wait_ge(aW, 2)
        for s_i in range(4):
            t.wait_ge(aTB, max(1, s_i + 1))
            t.wait_ge(s_IO, 16 * (s_i + 1))
            if s_i == 3:
                t.wait_ge(dVE, 24)
            elif s_i >= 1:
                t.wait_ge(dKE, SLAB_R0[s_i])
            t.matmul(P3[0:64, 512:1024], srf(INVF, 64), srf(IOF, 512),
                     start=True, stop=True).then_inc(pTB, 1)
            W(t, dST, s_i)
            W(t, dAP, 32 * s_i)
            for c in range(NC32):
                t.wait_ge(aSQ, 32 * s_i + c + 1)
                t.matmul(P2[:, 0:512], ones128[:], slb(c, 0, 512),
                         start=(c == 0), stop=(c == 31))
                mm = t.matmul(P3[:, 0:512], ones128[:],
                              psm((c % 2) * 1024, 512),
                              start=(c == 0), stop=(c == 31))
                mm.then_inc(pS, 1)
            t.wait_ge(dRS, s_i + 1)
            t.matmul(P2[:, 0:512], srf(ONES1, 128), srf(MEAN, 512),
                     start=True, stop=True)
            t.matmul(P3[:, 0:512], srf(ONES1, 128), srf(RSTD, 512),
                     start=True, stop=True).then_inc(pBC, 1)
            first_odd = True
            for (_s, kind, m, uG, ur) in SCHED_BY_SLAB[s_i]:
                pc = CONS.get(uG - 2)
                if pc is not None:
                    t.wait_ge({'dKE': dKE, 'dVE': dVE}[pc[0]], pc[1])
                if uG % 2 == 0:
                    banks = (P1[:, 0:512], P1[:, 512:1024])
                else:
                    if first_odd:
                        t.wait_ge(aTB, s_i + 2)
                        first_odd = False
                    banks = (P2[:, 512:1024], P3[:, 512:1024])
                for c in range(NC32):
                    if c == 0:
                        t.wait_ge(g_W0, 16 * (uG + 1))
                    if c == 16:
                        t.wait_ge(g_W1, 16 * (uG + 1))
                    t.wait_ge(dAP, 32 * s_i + c + 1)
                    mm = t.matmul(banks[c % 2],
                                  WB[:, (2 * uG + c // 16) % 2 * 2048
                                     + (c % 16) * 128:
                                     (2 * uG + c // 16) % 2 * 2048
                                     + (c % 16) * 128 + 128],
                                  slb(c, 0, 512),
                                  start=(c < 2), stop=(c >= 30))
                    if c == 15 or c == 31:
                        mm.then_inc(pWH, 1)
                if kind == 'V':
                    for tc in range(4):
                        vt = 4 * ur + tc
                        t.wait_ge(dVE, ur + 1)
                        W(t, aVT, vt - 1)
                        t.transpose(P5[:, (vt % 2) * 1024:(vt % 2) * 1024 + 128],
                                    psm(512 + (ur % 2) * 1024 + tc * 128, 128),
                                    idn_sb[:]).then_inc(pVT, 1)

        t.wait_ge(dRP, NR)
        t.wait_ge(aVT, 4 * NV)
        t.wait_ge(s_ld, 16 * NCONST)

        def emit_lav(h, items, k):
            sp, cid, bnd = items[k]
            first = (k == 0) or (items[k - 1][0] != sp)
            off = 128 * bnd if (bnd is not None and bnd > 0) else 0
            ko = 20 * h + k
            g_ = 2 * h + sp
            t.wait_ge(aE, ko + 1)
            po = (g_ % 2) * 512
            if first:
                W(t, dREC, g_ - 1)
            t.matmul(P3[:, po + off:po + 512], ones128[:],
                     psm((k % 2) * 1024 + off, 512 - off),
                     start=first, stop=(bnd == 3))
            if first:
                W(t, dNR, g_ - 1)
            mm = t.matmul(P2[:, po + off:po + 512],
                          VSB[:, cid * 1024 + (h // G) * 128:
                              cid * 1024 + (h // G) * 128 + 128],
                          psm((k % 2) * 1024 + off, 512 - off),
                          start=first, stop=(bnd == 3))
            mm.then_inc(pAV, 1)

        idx_t = t.value_load(sel_sb[0:1, 0:1])
        for j in nc.Switch(engines=[t], index=[idx_t], n=2):
            items = csched(j)
            for h in range(H):
                kv = h // G
                for k, (sp, cid, bnd) in enumerate(items):
                    ko = 20 * h + k
                    off = 128 * bnd if (bnd is not None and bnd > 0) else 0
                    W(t, aE, ko - 1)
                    mm = t.matmul(P1[:, (k % 2) * 512 + off:(k % 2) * 512 + 512],
                                  KT[:, kv * T + cid * 128: kv * T + cid * 128 + 128],
                                  qa(h, sp * 512 + off, 512 - off),
                                  start=True, stop=(bnd is None))
                    if bnd is None:
                        mm.then_inc(pSc, 1)
                    else:
                        t.matmul(P1[:, (k % 2) * 512 + off:(k % 2) * 512 + 512],
                                 alw_sb[:],
                                 bsh_sb[:, 384 - 128 * bnd + off:
                                        384 - 128 * bnd + 512],
                                 start=False, stop=True).then_inc(pSc, 1)
                    if k >= 1:
                        emit_lav(h, items, k - 1)
                emit_lav(h, items, NCHUNK - 1)

        def d_sums(m):
            t.wait_ge(aCP, m + 1)
            for nt in range(2):
                t.matmul(P3[0:1, nt * 512: nt * 512 + 512], ones_bf[:],
                         SLB[:, (m % 2) * 1024 + nt * 512:
                             (m % 2) * 1024 + nt * 512 + 512],
                         start=(m == 0), stop=(m == 31))
            for nt in range(2):
                mm = t.matmul(P3[32:33, nt * 512: nt * 512 + 512], ones_bf[:],
                              SLB[:, 2048 + (m % 2) * 1024 + nt * 512:
                                  2048 + (m % 2) * 1024 + nt * 512 + 512],
                              start=False, stop=(m == 31),
                              skip_group_check=True)
                if nt == 1:
                    mm.then_inc(pS2, 1)

        t.wait_ge(dNR, 64)
        for m in range(32):
            W(t, dX2, m - 1)
            dpo = P2 if m % 2 == 0 else P1
            for c in range(NC32):
                if c == 0:
                    t.wait_ge(g_Wo0, 16 * (m + 1))
                if c == 16:
                    t.wait_ge(g_Wo1, 16 * (m + 1))
                for t_i in range(2):
                    mm = t.matmul(dpo[:, t_i * 512:(t_i + 1) * 512],
                                  WB[:, (2 * m + c // 16) % 2 * 2048
                                     + (c % 16) * 128:
                                     (2 * m + c // 16) % 2 * 2048
                                     + (c % 16) * 128 + 128],
                                  qa(c, t_i * 512, 512),
                                  start=(c == 0), stop=(c == 31))
                    if t_i == 1 and (c == 15 or c == 31):
                        mm.then_inc(pD, 1)
            if m >= 1:
                d_sums(m - 1)
        d_sums(31)
        t.wait_ge(dRS2, 1)
        t.wait_ge(dX2, 32)
        for nt in range(2):
            t.matmul(P2[:, nt * 512:(nt + 1) * 512], srf(ONES1, 128),
                     SRF[0:1, nt * 512: nt * 512 + 512], start=True, stop=True)
        for nt in range(2):
            mm = t.matmul(P3[:, nt * 512:(nt + 1) * 512], srf(ONES1, 128),
                          SRF[0:1, 2048 + nt * 512: 2048 + nt * 512 + 512],
                          start=True, stop=True)
            if nt == 1:
                mm.then_inc(pBC2, 1)
        for k2 in range(2):
            for t_i in range(2):
                u = 2 * k2 + t_i
                W(t, dXA, u - 1)
                for c in range(NC32):
                    if c == 0:
                        t.wait_ge(g_A, 32 * (k2 + 1))
                    t.wait_ge(dAPE, c + 1)
                    mm = t.matmul(P1[:, (u % 2) * 512:(u % 2) * 512 + 512],
                                  WB[:, (c // 16) * 2048 + (c % 16) * 128:
                                     (c // 16) * 2048 + (c % 16) * 128 + 128],
                                  qa(c, t_i * 512, 512),
                                  start=(c == 0), stop=(c == 31))
                    if c == 31:
                        mm.then_inc(pXA, 1)
        for m in range(32):
            if m == 0:
                t.wait_ge(dXA, 4)
            W(t, aF, m)
            for c in range(NC32):
                if c == 0:
                    t.wait_ge(g_Wf0, 16 * (m + 1))
                if c == 16:
                    t.wait_ge(g_Wf1, 16 * (m + 1))
                for t_i in range(2):
                    mm = t.matmul(P2[:, t_i * 512:(t_i + 1) * 512],
                                  WB[:, (2 * m + c // 16) % 2 * 2048
                                     + (c % 16) * 128:
                                     (2 * m + c // 16) % 2 * 2048
                                     + (c % 16) * 128 + 128],
                                  qa(c, t_i * 512, 512),
                                  start=(c == 0), stop=False)
                    if c == 15 and t_i == 1:
                        mm.then_inc(pF, 1)
            t.wait_ge(g_BT0 if m % 2 == 0 else g_BT1, 16 * (m // 2 + 1))
            for t_i in range(2):
                for k2 in range(2):
                    mm = t.matmul(P2[:, t_i * 512:(t_i + 1) * 512],
                                  BT[:, (m % 2) * 256 + k2 * 128:
                                     (m % 2) * 256 + k2 * 128 + 128],
                                  SLB[:, k2 * 1024 + t_i * 512:
                                      k2 * 1024 + t_i * 512 + 512],
                                  start=False, stop=(k2 == 1))
                    if t_i == 1 and k2 == 1:
                        mm.then_inc(pF, 1)

    @block.scalar
    def _(sc_):
        sc_.wait_ge(s_ld, 16 * NCONST)
        sc_.activation(srf(INVF, 64), i64_sb[:], AF.Exp, bias=zf,
                       scale=np.float32(-np.log(10000.0) / 64.0)).then_inc(aTB, 1)
        sc_.wait_ge(pW, 2)
        sc_.activation(wexp[:, 0:1], P3[:, 0:1], AF.Identity, bias=zf,
                       scale=onef)
        sc_.activation(wexp[:, 1:2], P3[:, 1:2], AF.Identity, bias=zf,
                       scale=onef).then_inc(aW, 2)
        for s_i in range(4):
            sc_.wait_ge(pTB, s_i + 1)
            W(sc_, dRP, SLAB_R0[s_i])
            sc_.activation(PFG[0:64, 2048:2560], P3[0:64, 512:1024],
                           AF.Identity, bias=zf, scale=onef).then_inc(aTB, 1)
            for c in range(NC32):
                u = 32 * s_i + c
                W(sc_, pS, u - 1)
                if c % 16 == 0:
                    sc_.wait_ge(g_X0 if c < 16 else g_X1,
                                256 * (s_i + 1))
                sc_.square(psm((c % 2) * 1024, 512),
                           slb(c, 0, 512)).then_inc(aSQ, 1)
            sc_.wait_ge(pS, 32 * (s_i + 1))
            sc_.activation(srf(MEAN, 512), P2[0:1, 0:512], AF.Identity,
                           bias=zf, scale=np.float32(1.0 / 4096.0))
            sc_.activation(srf(SSD, 512), P3[0:1, 0:512], AF.Identity,
                           bias=zf, scale=np.float32(1.0 / 4096.0))
            sc_.drain()
            sc_.square(srf(VTMP, 512), srf(MEAN, 512)).then_inc(aST, 1)
            sc_.wait_ge(dST, s_i + 1)
            sc_.activation(srf(SSD, 512), srf(SVAR, 512), AF.Sqrt,
                           bias=np.float32(EPS), scale=onef).then_inc(aST, 1)
            for (_s, kind, m, uG, ur) in SCHED_BY_SLAB[s_i]:
                sc_.wait_ge(pWH, 2 * (uG + 1))
                psrc = (P1[:, 0:512] if uG % 2 == 0 else P2[:, 512:1024])
                if kind == 'V':
                    W(sc_, pVT, 4 * ur - 4)
                    sc_.activation(psm(512 + (ur % 2) * 1024, 512), psrc,
                                   AF.Identity, bias=zf,
                                   scale=onef).then_inc(aCE, 1)
                    for tc in range(4):
                        vt = 4 * ur + tc
                        sc_.wait_ge(pVT, vt + 1)
                        sc_.activation(VSB[:, (4 * s_i + tc) * 1024 + m * 128:
                                           (4 * s_i + tc) * 1024 + m * 128 + 128],
                                       P5[:, (vt % 2) * 1024:(vt % 2) * 1024 + 128],
                                       AF.Identity, bias=zf,
                                       scale=onef).then_inc(aVT, 1)
                else:
                    sc_.wait_ge(dTB, s_i + 1)
                    W(sc_, dRP, ur - 1)
                    sc_.activation(pfg(2048 + (ur % 2) * 512, 512), psrc,
                                   AF.Identity, bias=zf,
                                   scale=onef).then_inc(aCE, 1)
        idx_s = sc_.value_load(sel_sb[0:1, 0:1])
        for j in nc.Switch(engines=[sc_], index=[idx_s], n=2):
            items = csched(j)
            for h in range(H):
                for k, (sp, cid, bnd) in enumerate(items):
                    ko = 20 * h + k
                    off = 128 * bnd if (bnd is not None and bnd > 0) else 0
                    sc_.wait_ge(pSc, ko + 1)
                    W(sc_, pAV, ko - 1)
                    sc_.activation(psm((k % 2) * 1024 + off, 512 - off),
                                   P1[:, (k % 2) * 512 + off:
                                      (k % 2) * 512 + 512],
                                   AF.Exp, bias=np.float32(EXP_B),
                                   scale=np.float32(SCL)).then_inc(aE, 1)
        for m in range(32):
            sc_.wait_ge(dX2, m + 1)
            W(sc_, pS2, m - 1)
            sc_.activation(SLB[:, (m % 2) * 1024: (m % 2) * 1024 + 1024],
                           pfg((m % 2) * 1024, 1024),
                           AF.Identity, bias=zf, scale=onef)
            sc_.square(SLB[:, 2048 + (m % 2) * 1024: 2048 + (m % 2) * 1024 + 1024],
                       pfg((m % 2) * 1024, 1024)).then_inc(aCP, 1)
        sc_.wait_ge(pS2, 32)
        sc_.activation(SRF[0:1, 0:512], P3[0:1, 0:512], AF.Identity,
                       bias=zf, scale=np.float32(1.0 / 4096.0))
        sc_.activation(SRF[0:1, 512:1024], P3[0:1, 512:1024], AF.Identity,
                       bias=zf, scale=np.float32(1.0 / 4096.0))
        sc_.activation(SRF[0:1, 2048:2560], P3[32:33, 0:512], AF.Identity,
                       bias=zf, scale=np.float32(1.0 / 4096.0))
        sc_.activation(SRF[0:1, 2560:3072], P3[32:33, 512:1024], AF.Identity,
                       bias=zf, scale=np.float32(1.0 / 4096.0))
        sc_.drain()
        sc_.square(SRF[0:1, 1024:1536], SRF[0:1, 0:512])
        sc_.square(SRF[0:1, 1536:2048], SRF[0:1, 512:1024]).then_inc(aST2, 1)
        sc_.wait_ge(dST2, 1)
        sc_.activation(SRF[0:1, 1024:1536], SRF[0:1, 2048:2560], AF.Sqrt,
                       bias=np.float32(EPS), scale=onef)
        sc_.activation(SRF[0:1, 1536:2048], SRF[0:1, 2560:3072], AF.Sqrt,
                       bias=np.float32(EPS), scale=onef).then_inc(aST2, 1)
        for m in range(32):
            sc_.wait_ge(pF, 2 * (m + 1))
            W(sc_, s_OUT0 if m % 2 == 0 else s_OUT1, 16 * (m // 2))
            sc_.activation(pfg((m % 2) * 1024, 1024), P2[:, 0:1024],
                           AF.Identity, bias=bias_sb[:, m:m + 1],
                           scale=onef).then_inc(aF, 1)

    class _DrainVec:
        _OPS = {"memset", "tensor_copy", "tensor_tensor", "tensor_add",
                "tensor_sub", "tensor_mul", "tensor_max", "tensor_scalar",
                "tensor_single_scalar", "tensor_scalar_add",
                "tensor_scalar_sub", "tensor_scalar_mul", "tensor_scalar_max",
                "tensor_scalar_min", "scalar_tensor_tensor", "reciprocal"}

        def __init__(self, eng):
            self._e = eng

        def __getattr__(self, n):
            a = getattr(self._e, n)
            if n in _DrainVec._OPS:
                def wrap(*args, **kw):
                    inst = a(*args, **kw)
                    self._e.drain()
                    return inst
                return wrap
            return a

    @block.vector
    def _(v_raw):
        v = _DrainVec(v_raw)
        v.memset(ones_bf[:], 1.0)
        v.memset(ones128[:], 1.0)
        v.memset(srf(ONES1, 128), 1.0)
        v.wait_ge(s_ld, 16 * NCONST)
        v.tensor_mul(wvec[:], cf_sb[:], sc_sb[:]).then_inc(dCN, 3)
        for s_i in range(4):
            v.wait_ge(s_TB, 32 * s_i)
            ang = PFG[0:64, 2048:2560]
            angr = PFG[0:64, 2560:3072]
            u_t = PFG[0:64, 3072:3584]
            q_t = PFG[0:64, 3584:4096]
            sinp = PFG[0:64, 1024:1536]
            v.wait_ge(aTB, s_i + 2)
            v.tensor_scalar(u_t, ang, INV2PI, MAGIC, ALU.mult, ALU.add)
            v.tensor_scalar_add(u_t, u_t, -MAGIC)
            v.scalar_tensor_tensor(angr, u_t, NEG2PI, ang, ALU.mult, ALU.add)
            v.tensor_mul(u_t, angr, angr)
            v.tensor_scalar_mul(ang, angr, SIN_LEAD)
            v.tensor_scalar_add(q_t, u_t, SIN_D[4])
            for k in (3, 2, 1):
                v.scalar_tensor_tensor(q_t, q_t, SIN_D[k], u_t, ALU.add, ALU.mult)
            v.scalar_tensor_tensor(sinp, q_t, SIN_D[0], ang, ALU.add, ALU.mult)
            v.tensor_scalar_mul(PFG[0:64, 512:1024], sinp, -1.0)
            v.tensor_scalar_add(q_t, u_t, COS_D[5])
            for k in (4, 3, 2, 1):
                v.scalar_tensor_tensor(q_t, q_t, COS_D[k], u_t, ALU.add, ALU.mult)
            v.tensor_scalar(PFG[0:64, 0:512], q_t, COS_D[0], COS_LEAD,
                            ALU.add, ALU.mult).then_inc(dTB, 1)
            v.wait_ge(aST, 2 * s_i + 1)
            v.tensor_sub(srf(SVAR, 512), srf(SSD, 512),
                         srf(VTMP, 512)).then_inc(dST, 1)
            v.wait_ge(aST, 2 * s_i + 2)
            v.reciprocal(srf(RSTD, 512), srf(SSD, 512)).then_inc(dRS, 1)
            v.wait_ge(pBC, s_i + 1)
            v.wait_ge(s_TB, 32 * (s_i + 1))
            for c in range(NC32):
                t1 = pfg(1024 + (c % 2) * 512, 512)
                v.tensor_sub(t1, slb(c, 0, 512), P2[:, 0:512])
                v.tensor_mul(slb(c, 0, 512), t1, P3[:, 0:512]).then_inc(dAP, 1)
            v.wait_ge(s_TB, 32 * (s_i + 1))

            def rope(kind, m, ur):
                v.wait_ge(s_RH0 if ur % 2 == 0 else s_RH1, 32 * (ur // 2 + 1))
                stg = pfg(2048 + (ur % 2) * 512, 512)
                rh = pfg(3072 + (ur % 2) * 512, 512)
                t1 = pfg(1024, 512)
                t2 = pfg(1536, 512)
                v.tensor_mul(t1, stg, pfg(0, 512))
                v.tensor_mul(t2, rh, pfg(512, 512))
                if kind == 'K':
                    dstp = KT[:, m * T + s_i * 512: m * T + s_i * 512 + 512]
                else:
                    dstp = qa(m, s_i * 512, 512)
                v.tensor_add(dstp, t1, t2).then_inc(dRP, 1)

            prevKQ = None
            for (_s, kind, m, uG, ur) in SCHED_BY_SLAB[s_i]:
                bankB = (P1[:, 512:1024] if uG % 2 == 0 else P3[:, 512:1024])
                v.wait_ge(aCE, uG + 1)
                if kind == 'V':
                    stv = psm(512 + (ur % 2) * 1024, 512)
                    v.tensor_tensor(stv, stv, bankB, ALU.add).then_inc(dVE, 1)
                    continue
                stg = pfg(2048 + (ur % 2) * 512, 512)
                v.tensor_tensor(stg, stg, bankB, ALU.add).then_inc(dKE, 1)
                if prevKQ is not None:
                    rope(*prevKQ)
                prevKQ = (kind, m, ur)
            if prevKQ is not None:
                rope(*prevKQ)
        idx_v = v_raw.value_load(sel_sb[0:1, 0:1])
        for j in nc.Switch(engines=[v_raw], index=[idx_v], n=2):
            for h in range(H):
                for g_l, endk in span_ends(j):
                    g_ = 2 * h + g_l
                    po = (g_ % 2) * 512
                    v.wait_ge(pAV, 20 * h + endk + 1)
                    v.reciprocal(pfg(2048 + (g_ % 2) * 512, 512),
                                 P3[:, po:po + 512]).then_inc(dREC, 1)
                    v.tensor_mul(qa(h, g_l * 512, 512), P2[:, po:po + 512],
                                 pfg(2048 + (g_ % 2) * 512, 512)) \
                        .then_inc(dNR, 1)
        for m in range(32):
            v.wait_ge(pD, 2 * (m + 1))
            v.wait_ge(s_X0 if m % 2 == 0 else s_X1, 16 * (m // 2 + 1))
            W(v, s_SC0 if m % 2 == 0 else s_SC1, 16 * (m // 2))
            W(v, aCP, m - 1)
            v.tensor_add(pfg((m % 2) * 1024, 1024),
                         (P2 if m % 2 == 0 else P1)[:, 0:1024],
                         pfg(2048 + (m % 2) * 1024, 1024)).then_inc(dX2, 1)
        v.wait_ge(aST2, 1)
        v.tensor_sub(SRF[0:1, 2048:2560], SRF[0:1, 2048:2560],
                     SRF[0:1, 1024:1536])
        v.tensor_sub(SRF[0:1, 2560:3072], SRF[0:1, 2560:3072],
                     SRF[0:1, 1536:2048]).then_inc(dST2, 1)
        v.wait_ge(aST2, 2)
        v.reciprocal(SRF[0:1, 2048:2560], SRF[0:1, 1024:1536])
        v.reciprocal(SRF[0:1, 2560:3072], SRF[0:1, 1536:2048]).then_inc(dRS2, 1)
        v.wait_ge(pBC2, 1)
        v.wait_ge(pD, 64)
        v.wait_ge(s_SC0, 16 * 16)
        v.wait_ge(s_SC1, 16 * 16)
        for c in range(NC32):
            v.wait_ge(g_X20 if c % 2 == 0 else g_X21, 16 * (c // 2 + 1))
            t1 = pfg((c % 2) * 1024, 1024)
            v.tensor_sub(t1, SLB[:, 6144 + (c % 8) * 1024:
                                 6144 + (c % 8) * 1024 + 1024], P2[:, 0:1024])
            v.tensor_mul(qa(c, 0, OWN), t1, P3[:, 0:1024]).then_inc(dAPE, 1)
        for k2 in range(2):
            for t_i in range(2):
                u = 2 * k2 + t_i
                v.wait_ge(pXA, u + 1)
                v.tensor_scalar_mul(SLB[:, k2 * 1024 + t_i * 512:
                                        k2 * 1024 + t_i * 512 + 512],
                                    P1[:, (u % 2) * 512:(u % 2) * 512 + 512],
                                    wexp[:, k2:k2 + 1]).then_inc(dXA, 1)


    nc._kernel_ctx = ctx
    return nc


def _lhsT_tiles(Wdo):
    Din, Dout = Wdo.shape
    return np.ascontiguousarray(
        Wdo.reshape(Din // 128, 128, Dout // 128, 128).transpose(2, 1, 0, 3))


def _consts():
    alow = np.zeros((128, 128), np.float32)
    for m_ in range(128):
        alow[m_, m_:] = -1e30
    bshv = np.zeros((128, 896), np.float32)
    for zc in range(896):
        z = zc - 384
        r = z + 1
        if r < 0:
            bshv[0, zc] = 1.0
        elif r < 128:
            bshv[r, zc] = 1.0
    e16v = np.zeros((2, 16, 128), np.float32)
    for k2 in range(2):
        for p_ in range(128):
            e16v[k2, (128 * k2 + p_) >> 4, p_] = 1.0
    idnv = np.eye(128, dtype=np.float32)
    return (alow.astype(ml_dtypes.bfloat16), bshv.astype(ml_dtypes.bfloat16),
            e16v, idnv.astype(ml_dtypes.bfloat16))


def kernel(L0_attn_X, L0_attn_Wq, L0_attn_Wk, L0_attn_Wv, L0_attn_Wo,
           L0_ffn_W, L0_ffn_bias, L0_ffn_A, L0_ffn_Bf, L0_ffn_coeffs,
           L0_ffn_scales):
    X = np.asarray(L0_attn_X, np.float32)
    wq_t = _lhsT_tiles(np.asarray(L0_attn_Wq, np.float32))
    wk_t = _lhsT_tiles(np.asarray(L0_attn_Wk, np.float32))
    wv_t = _lhsT_tiles(np.asarray(L0_attn_Wv, np.float32))
    wo_t = _lhsT_tiles(np.asarray(L0_attn_Wo, np.float32))
    wf_t = _lhsT_tiles(np.ascontiguousarray(np.asarray(L0_ffn_W, np.float32).T))
    A2 = np.asarray(L0_ffn_A, np.float32).reshape(256, 4096)
    wa_t = _lhsT_tiles(np.ascontiguousarray(A2.T))
    Bcat = np.ascontiguousarray(
        np.transpose(np.asarray(L0_ffn_Bf, np.float32), (0, 2, 1)).reshape(256, 4096))
    wb_t = _lhsT_tiles(Bcat)
    bias_t = np.ascontiguousarray(
        np.asarray(L0_ffn_bias, np.float32).reshape(32, 128).T)
    cf_v = np.asarray(L0_ffn_coeffs, np.float32).reshape(16, 1)
    sc_v = np.asarray(L0_ffn_scales, np.float32).reshape(16, 1)
    alow, bshv, e16v, idnv = _consts()
    i64_v = np.arange(64, dtype=np.float32).reshape(1, 64)

    shared = dict(wq=wq_t, wk=wk_t, wv=wv_t, wo=wo_t, wf=wf_t, wa=wa_t,
                  wb=wb_t, bi=bias_t, cf=cf_v, scl=sc_v, e16=e16v,
                  alw=alow, bsh=bshv, idn=idnv, i64=i64_v)

    in_maps = []
    spans = []
    for core in range(8):
        b_, j_ = divmod(core, 2)
        a0 = 512 * j_
        b0 = 1536 - 512 * j_
        own = list(range(a0, a0 + 512)) + list(range(b0, b0 + 512))
        other = [p for p in range(T) if not (a0 <= p < a0 + 512
                                             or b0 <= p < b0 + 512)]
        perm = np.array(own + other, dtype=np.int64)
        spans.append((b_, a0, b0))
        XT = np.ascontiguousarray(X[b_].T[:, perm])
        iov = perm.astype(np.float32).reshape(1, T)
        m = dict(shared)
        m["xf"] = XT
        m["io"] = iov
        m["sel"] = np.array([[j_]], np.int32)
        in_maps.append(m)

    nc = build_program()
    res = run_bass_kernel_spmd(nc, in_maps, core_ids=list(range(8)))

    outp = np.empty((B, T, D), np.float32)
    for core in range(8):
        b_, a0, b0 = spans[core]
        o = res.results[core]["out"]
        outp[b_, a0:a0 + 512, :] = o[:, 0:512].T
        outp[b_, b0:b0 + 512, :] = o[:, 512:1024].T
    return outp


# version 13
# speedup vs baseline: 1.1473x; 1.1473x over previous
import numpy as np
import ml_dtypes
import concourse.bass as bass
import concourse.mybir as mybir
from concourse.bass_utils import run_bass_kernel_spmd
from contextlib import ExitStack

F32 = mybir.dt.float32
BF16 = mybir.dt.bfloat16
I32 = mybir.dt.int32
AF = mybir.ActivationFunctionType
ALU = mybir.AluOpType

B, T, D = 4, 2048, 4096
H, HKV, DH, G = 32, 8, 128, 4
NC32 = 32
OWN = 1024
EPS = 1e-5
SCL = float(1.0 / np.sqrt(128.0))
EXP_B = -12.0
MAGIC = float(3 * 2 ** 22)
INV2PI = float(1.0 / (2 * np.pi))
NEG2PI = float(-2.0 * np.pi)
NCHUNK = 20
NCONST = 9


def _fit_poly():
    dom = [0.0, (np.pi * 1.001) ** 2]
    u = np.linspace(dom[0], dom[1], 20001)
    x = np.sqrt(u)
    fs = np.where(x == 0, 1.0, np.sin(x) / np.where(x == 0, 1, x))
    cs = np.polynomial.chebyshev.Chebyshev.fit(u, fs, 5, domain=dom) \
        .convert(kind=np.polynomial.Polynomial).coef
    cc = np.polynomial.chebyshev.Chebyshev.fit(u, np.cos(x), 6, domain=dom) \
        .convert(kind=np.polynomial.Polynomial).coef
    return cs, cc


_PS, _PC = _fit_poly()
SIN_LEAD = float(_PS[-1]); SIN_D = [float(c) for c in (_PS / _PS[-1])[:-1]]
COS_LEAD = float(_PC[-1]); COS_D = [float(c) for c in (_PC / _PC[-1])[:-1]]


def csched(j):
    if j == 0:
        a = [(0, c, c) for c in range(4)]
        b = [(1, c, None) for c in [8, 9, 10, 11, 12, 13, 14, 15, 0, 1, 2, 3]] \
            + [(1, 4 + i, i) for i in range(4)]
    else:
        a = [(0, c, None) for c in [8, 9, 10, 11]] + [(0, i, i) for i in range(4)]
        b = [(1, c, None) for c in [8, 9, 10, 11, 0, 1, 2, 3]] \
            + [(1, 4 + i, i) for i in range(4)]
    return a + b


def build_sched():
    sched = []
    uG = uR = uV = 0
    slab_g0 = []
    slab_r0 = []
    for s in range(4):
        slab_g0.append(uG)
        slab_r0.append(uR)
        for m in range(16):
            if m < 8:
                sched.append((s, 'K', m, uG, uR)); uR += 1
            else:
                sched.append((s, 'V', m - 8, uG, uV)); uV += 1
            uG += 1
        if s < 2:
            for m in range(32):
                sched.append((s, 'Q', m, uG, uR)); uR += 1
                uG += 1
    return sched, slab_g0, slab_r0, uG, uR, uV


SCHED, SLAB_G0, SLAB_R0, NG, NR, NV = build_sched()
SCHED_BY_SLAB = [[e for e in SCHED if e[0] == s] for s in range(4)]
CONS = {}
for (s, kind, m, uG, ur) in SCHED:
    CONS[uG] = ('aVE', ur + 1) if kind == 'V' else ('aKE', ur + 1)


def build_program():
    nc = bass.Bass("TRN2", target_bir_lowering=False, debug=False)

    xf = nc.dram_tensor("xf", [D, T], F32, kind="ExternalInput").ap()
    io = nc.dram_tensor("io", [1, T], F32, kind="ExternalInput").ap()
    i64 = nc.dram_tensor("i64", [1, 64], F32, kind="ExternalInput").ap()
    sel = nc.dram_tensor("sel", [1, 1], I32, kind="ExternalInput").ap()
    wq = nc.dram_tensor("wq", [32, 128, 32, 128], F32, kind="ExternalInput").ap()
    wk = nc.dram_tensor("wk", [8, 128, 32, 128], F32, kind="ExternalInput").ap()
    wv = nc.dram_tensor("wv", [8, 128, 32, 128], F32, kind="ExternalInput").ap()
    wo = nc.dram_tensor("wo", [32, 128, 32, 128], F32, kind="ExternalInput").ap()
    wf = nc.dram_tensor("wf", [32, 128, 32, 128], F32, kind="ExternalInput").ap()
    wa = nc.dram_tensor("wa", [2, 128, 32, 128], F32, kind="ExternalInput").ap()
    wb = nc.dram_tensor("wb", [32, 128, 2, 128], F32, kind="ExternalInput").ap()
    bi = nc.dram_tensor("bi", [128, 32], F32, kind="ExternalInput").ap()
    cf = nc.dram_tensor("cf", [16, 1], F32, kind="ExternalInput").ap()
    scl = nc.dram_tensor("scl", [16, 1], F32, kind="ExternalInput").ap()
    e16 = nc.dram_tensor("e16", [2, 16, 128], F32, kind="ExternalInput").ap()
    alw = nc.dram_tensor("alw", [128, 128], BF16, kind="ExternalInput").ap()
    bsh = nc.dram_tensor("bsh", [128, 896], BF16, kind="ExternalInput").ap()
    idn = nc.dram_tensor("idn", [128, 128], BF16, kind="ExternalInput").ap()
    out = nc.dram_tensor("out", [D, OWN], F32, kind="ExternalOutput").ap()
    x2scr = nc.dram_tensor("x2scr", [D, OWN], F32, kind="Internal").ap()

    ctx = ExitStack()
    sb = lambda n, s, d: ctx.enter_context(nc.sbuf_tensor(n, s, d))
    pst = lambda n, s, d: ctx.enter_context(nc.psum_tensor(n, s, d))
    sm = lambda n: ctx.enter_context(nc.semaphore(n))

    QA = sb("QA", [128, H * OWN], BF16)
    SLB = sb("SLB", [128, NC32 * 512], BF16)
    KT = sb("KT", [128, HKV * T], BF16)
    VSB = sb("VSB", [128, 16 * 1024], BF16)
    WB = sb("WB", [128, 4096], BF16)
    BT = sb("BT", [128, 512], BF16)
    PFG = sb("PFG", [128, 4096], F32)
    PSM = sb("PSM", [128, 2048], BF16)
    SRF = sb("SRF", [1, 3328], F32)
    alw_sb = sb("alw_sb", [128, 128], BF16)
    bsh_sb = sb("bsh_sb", [128, 896], BF16)
    idn_sb = sb("idn_sb", [128, 128], BF16)
    e16_sb = sb("e16_sb", [16, 256], F32)
    bias_sb = sb("bias_sb", [128, 32], F32)
    cf_sb = sb("cf_sb", [16, 1], F32)
    sc_sb = sb("sc_sb", [16, 1], F32)
    wvec = sb("wvec", [16, 1], F32)
    wexp = sb("wexp", [128, 2], F32)
    ones_bf = sb("ones_bf", [128, 1], BF16)
    ones128 = sb("ones128", [128, 128], BF16)
    sel_sb = sb("sel_sb", [1, 1], I32)
    i64_sb = sb("i64_sb", [1, 64], F32)

    def srf(lo, n):
        return SRF[0:1, lo: lo + n]
    IOF, MEAN, RSTD, SVAR, SSD, VTMP, ONES1, INVF = \
        0, 512, 1024, 1536, 2048, 2560, 3072, 3200

    def pfg(lo, n):
        return PFG[:, lo: lo + n]

    def psm(lo, n):
        return PSM[:, lo: lo + n]

    def slb(c, lo, n):
        return SLB[:, c * 512 + lo: c * 512 + lo + n]

    def qa(h, lo, n):
        return QA[:, h * OWN + lo: h * OWN + lo + n]

    P1 = pst("P1", [128, 1024], F32)
    P2 = pst("P2", [128, 1024], F32)
    P3 = pst("P3", [128, 1024], F32)
    P5 = pst("P5", [128, 2048], BF16)

    s_ld = sm("s_ld"); s_IO = sm("s_IO"); s_TB = sm("s_TB")
    s_RH0 = sm("s_RH0"); s_RH1 = sm("s_RH1")
    s_X0 = sm("s_X0"); s_X1 = sm("s_X1")
    s_SC0 = sm("s_SC0"); s_SC1 = sm("s_SC1")
    s_X20 = sm("s_X20"); s_X21 = sm("s_X21")
    s_OUT0 = sm("s_OUT0"); s_OUT1 = sm("s_OUT1")
    s_CP0 = sm("s_CP0"); s_CP1 = sm("s_CP1")
    g_X0 = sm("g_X0"); g_X1 = sm("g_X1"); g_A = sm("g_A")
    g_W0 = sm("g_W0"); g_W1 = sm("g_W1")
    g_Wo0 = sm("g_Wo0"); g_Wo1 = sm("g_Wo1")
    g_Wf0 = sm("g_Wf0"); g_Wf1 = sm("g_Wf1")
    g_BT0 = sm("g_BT0"); g_BT1 = sm("g_BT1")
    g_X20 = sm("g_X20"); g_X21 = sm("g_X21")
    pW = sm("pW"); pTB = sm("pTB"); pS = sm("pS"); pBC = sm("pBC")
    pWH = sm("pWH"); pVT = sm("pVT")
    pSc = sm("pSc"); pAV = sm("pAV"); pLB = sm("pLB")
    pD = sm("pD"); pS2 = sm("pS2"); pBC2 = sm("pBC2"); pXA = sm("pXA")
    pF = sm("pF")
    aTB = sm("aTB"); aW = sm("aW"); aSQ = sm("aSQ"); aST = sm("aST")
    aKE = sm("aKE"); aVE = sm("aVE"); aVT = sm("aVT")
    aE = sm("aE"); aOC = sm("aOC"); aCP = sm("aCP"); aST2 = sm("aST2")
    aF = sm("aF")
    dCN = sm("dCN"); dTB = sm("dTB"); dST = sm("dST"); dRS = sm("dRS")
    dAP = sm("dAP"); dRP = sm("dRP")
    dREC = sm("dREC"); dNR = sm("dNR")
    dX2 = sm("dX2"); dST2 = sm("dST2"); dRS2 = sm("dRS2")
    dAPE = sm("dAPE"); dXA = sm("dXA"); dOUT = sm("dOUT")

    def W(e, s, v):
        if v > 0:
            e.wait_ge(s, v)

    def span_ends(j):
        nA = 4 if j == 0 else 8
        return [(0, nA - 1), (1, NCHUNK - 1)]

    block = ctx.enter_context(nc.Block())

    zf = np.float32(0.0)
    onef = np.float32(1.0)

    @block.sync
    def _(s):
        s.dma_start(i64_sb[:], i64).then_inc(s_ld, 16)
        s.dma_start(sel_sb[:], sel).then_inc(s_ld, 16)
        s.dma_start(alw_sb[:], alw).then_inc(s_ld, 16)
        s.dma_start(bsh_sb[:], bsh).then_inc(s_ld, 16)
        s.dma_start(idn_sb[:], idn).then_inc(s_ld, 16)
        s.dma_start(e16_sb[:].rearrange("e (k c) -> e k c", k=2),
                    e16.rearrange("k e c -> e k c")).then_inc(s_ld, 16)
        s.dma_start(bias_sb[:], bi).then_inc(s_ld, 16)
        s.dma_start(cf_sb[:], cf).then_inc(s_ld, 16)
        s.dma_start(sc_sb[:], scl).then_inc(s_ld, 16)
        for s_i in range(4):
            W(s, pTB, s_i)
            s.dma_start(srf(IOF, 512),
                        io[0:1, s_i * 512:(s_i + 1) * 512]).then_inc(s_IO, 16)
            s.wait_ge(dTB, s_i + 1)
            s.dma_start(PFG[64:128, 0:512], PFG[0:64, 0:512]).then_inc(s_TB, 16)
            s.dma_start(PFG[64:128, 512:1024],
                        PFG[0:64, 1024:1536]).then_inc(s_TB, 16)
            r0 = SLAB_R0[s_i]
            nrope = 40 if s_i < 2 else 8
            for rl in range(nrope):
                r = r0 + rl
                s.wait_ge(aKE, r + 1)
                W(s, dRP, r - 1)
                stg = pfg(2048 + (r % 2) * 512, 512)
                rh = pfg(3072 + (r % 2) * 512, 512)
                rsem = s_RH0 if r % 2 == 0 else s_RH1
                s.dma_start(rh[0:64, :], stg[64:128, :]).then_inc(rsem, 16)
                s.dma_start(rh[64:128, :], stg[0:64, :]).then_inc(rsem, 16)
        s.wait_ge(dNR, 64)
        for m in range(32):
            W(s, dX2, m - 1)
            s.dma_start(pfg(2048 + (m % 2) * 1024, 1024),
                        xf[m * 128:(m + 1) * 128, 0:OWN]) \
                .then_inc(s_X0 if m % 2 == 0 else s_X1, 16)
            s.wait_ge(dX2, m + 1)
            W(s, s_SC0 if m % 2 == 0 else s_SC1, 16 * (m // 2))
            s.dma_start(x2scr[m * 128:(m + 1) * 128, :],
                        pfg((m % 2) * 1024, 1024)) \
                .then_inc(s_SC0 if m % 2 == 0 else s_SC1, 16)
        s.wait_ge(dX2, 32)
        for m in range(32):
            s.wait_ge(s_SC0 if m % 2 == 0 else s_SC1, 16 * (m // 2 + 1))
            W(s, s_OUT0 if m % 2 == 0 else s_OUT1, 16 * (m // 2))
            s.dma_start(out[m * 128:(m + 1) * 128, :],
                        x2scr[m * 128:(m + 1) * 128, :]) \
                .then_inc(s_CP0 if m % 2 == 0 else s_CP1, 16)
        s.wait_ge(s_OUT0, 16 * 16)
        s.wait_ge(s_OUT1, 16 * 16)

    @block.gpsimd
    def _(g):
        xfr = xf.rearrange("(c p) t -> p c t", p=128)
        x2r = x2scr.rearrange("(c p) t -> p c t", p=128)
        for s_i in range(4):
            for c in range(NC32):
                if s_i >= 1 and c % 16 == 0:
                    W(g, pWH, 2 * SLAB_G0[s_i] - 1 + c // 16)
                g.dma_start(slb(c, 0, 512),
                            xfr[:, c, s_i * 512:(s_i + 1) * 512]) \
                    .then_inc(g_X0 if c < 16 else g_X1, 16)
            for (_s, kind, m, uG, ur) in SCHED_BY_SLAB[s_i]:
                wsrc = {'K': wk, 'V': wv, 'Q': wq}[kind]
                for h in range(2):
                    w_ = 2 * uG + h
                    W(g, pWH, w_ - 1)
                    g.dma_start(
                        WB[:, h * 2048: h * 2048 + 2048]
                        .rearrange("p (k c) -> p k c", k=16),
                        wsrc[m, :, 16 * h:16 * h + 16, :]) \
                        .then_inc(g_W0 if h == 0 else g_W1, 16)
        g.wait_ge(pWH, 2 * NG)
        for m in range(32):
            for h in range(2):
                w_ = 2 * m + h
                W(g, pD, w_ - 1)
                g.dma_start(
                    WB[:, h * 2048: h * 2048 + 2048]
                    .rearrange("p (k c) -> p k c", k=16),
                    wo[m, :, 16 * h:16 * h + 16, :]) \
                    .then_inc(g_Wo0 if h == 0 else g_Wo1, 16)
            if 2 <= m < 10:
                c = m - 2
                W(g, s_SC0 if c % 2 == 0 else s_SC1, 16 * (c // 2 + 1))
                W(g, g_X20 if c % 2 == 0 else g_X21, 16 * (c // 2))
                g.dma_start(SLB[:, 6144 + (c % 8) * 1024:
                                6144 + (c % 8) * 1024 + 1024],
                            x2r[:, c, :]) \
                    .then_inc(g_X20 if c % 2 == 0 else g_X21, 16)
        g.wait_ge(pD, 64)
        for h in range(2):
            g.dma_start(
                WB[:, h * 2048: h * 2048 + 2048]
                .rearrange("p (k c) -> p k c", k=16),
                wa[0, :, 16 * h:16 * h + 16, :]).then_inc(g_A, 16)
        for c in range(8, NC32):
            W(g, dAPE, c - 7)
            W(g, s_SC0 if c % 2 == 0 else s_SC1, 16 * (c // 2 + 1))
            W(g, g_X20 if c % 2 == 0 else g_X21, 16 * (c // 2))
            g.dma_start(SLB[:, 6144 + (c % 8) * 1024:
                            6144 + (c % 8) * 1024 + 1024],
                        x2r[:, c, :]) \
                .then_inc(g_X20 if c % 2 == 0 else g_X21, 16)
        for h in range(2):
            if h == 0:
                g.wait_ge(pXA, 2)
            g.dma_start(
                WB[:, h * 2048: h * 2048 + 2048]
                .rearrange("p (k c) -> p k c", k=16),
                wa[1, :, 16 * h:16 * h + 16, :]).then_inc(g_A, 16)
        g.wait_ge(pXA, 4)
        for mm_ in range(34):
            if mm_ < 32:
                m = mm_
                for h in range(2):
                    w_ = 2 * m + h
                    W(g, pF, w_ - 1)
                    g.dma_start(
                        WB[:, h * 2048: h * 2048 + 2048]
                        .rearrange("p (k c) -> p k c", k=16),
                        wf[m, :, 16 * h:16 * h + 16, :]) \
                        .then_inc(g_Wf0 if h == 0 else g_Wf1, 16)
                W(g, pF, 2 * m - 2)
                g.dma_start(BT[:, (m % 2) * 256:(m % 2) * 256 + 256]
                            .rearrange("p (k c) -> p k c", k=2),
                            wb[m]).then_inc(g_BT0 if m % 2 == 0 else g_BT1, 16)
            if mm_ >= 2:
                m = mm_ - 2
                g.wait_ge(aF, m + 1)
                g.wait_ge(s_CP0 if m % 2 == 0 else s_CP1, 16 * (m // 2 + 1))
                g.dma_start(out[m * 128:(m + 1) * 128, :],
                            pfg((m % 2) * 1024, 1024),
                            accum_op=ALU.add) \
                    .then_inc(s_OUT0 if m % 2 == 0 else s_OUT1, 16)
        g.wait_ge(s_OUT0, 16 * 16)
        g.wait_ge(s_OUT1, 16 * 16)

    @block.tensor
    def _(t):
        t.wait_ge(dCN, 3)
        for k2 in range(2):
            t.matmul(P3[:, k2:k2 + 1], e16_sb[:, k2 * 128:(k2 + 1) * 128],
                     wvec[:], start=True, stop=True).then_inc(pW, 1)
        t.wait_ge(aW, 2)
        pendV = [None]

        def emit_vt(ur):
            for tc in range(4):
                vt = 4 * ur + tc
                t.wait_ge(aVE, ur + 1)
                W(t, aVT, vt - 1)
                t.transpose(P5[:, (vt % 2) * 1024:(vt % 2) * 1024 + 128],
                            psm(512 + (ur % 2) * 1024 + tc * 128, 128),
                            idn_sb[:]).then_inc(pVT, 1)

        for s_i in range(4):
            t.wait_ge(aTB, max(1, s_i + 1))
            t.wait_ge(s_IO, 16 * (s_i + 1))
            W(t, dAP, 32 * s_i)
            t.matmul(P3[0:64, 512:1024], srf(INVF, 64), srf(IOF, 512),
                     start=True, stop=True).then_inc(pTB, 1)
            if pendV[0] is not None:
                emit_vt(pendV[0])
                pendV[0] = None
            W(t, dST, s_i)
            W(t, dAP, 32 * s_i)
            for c in range(NC32):
                t.wait_ge(aSQ, 32 * s_i + c + 1)
                t.matmul(P2[:, 0:512], ones128[:], slb(c, 0, 512),
                         start=(c == 0), stop=(c == 31))
                mm = t.matmul(P3[:, 0:512], ones128[:],
                              psm((c % 2) * 1024, 512),
                              start=(c == 0), stop=(c == 31))
                mm.then_inc(pS, 1)
            t.wait_ge(dRS, s_i + 1)
            t.matmul(P2[:, 0:512], srf(ONES1, 128), srf(MEAN, 512),
                     start=True, stop=True)
            t.matmul(P2[:, 512:1024], srf(ONES1, 128), srf(MEAN, 512),
                     start=True, stop=True)
            t.matmul(P3[:, 0:512], srf(ONES1, 128), srf(RSTD, 512),
                     start=True, stop=True)
            t.wait_ge(aTB, s_i + 2)
            t.matmul(P3[:, 512:1024], srf(ONES1, 128), srf(RSTD, 512),
                     start=True, stop=True).then_inc(pBC, 1)
            for (_s, kind, m, uG, ur) in SCHED_BY_SLAB[s_i]:
                if pendV[0] is not None:
                    emit_vt(pendV[0])
                    pendV[0] = None
                pc = CONS.get(uG - 2)
                if pc is not None:
                    t.wait_ge({'aKE': aKE, 'aVE': aVE}[pc[0]], pc[1])
                dst = P1[:, (uG % 2) * 512:(uG % 2) * 512 + 512]
                for c in range(NC32):
                    if c == 0:
                        t.wait_ge(g_W0, 16 * (uG + 1))
                    if c == 16:
                        t.wait_ge(g_W1, 16 * (uG + 1))
                    t.wait_ge(dAP, 32 * s_i + 2 * (c // 2) + 2)
                    mm = t.matmul(dst,
                                  WB[:, (2 * uG + c // 16) % 2 * 2048
                                     + (c % 16) * 128:
                                     (2 * uG + c // 16) % 2 * 2048
                                     + (c % 16) * 128 + 128],
                                  slb(c, 0, 512),
                                  start=(c == 0), stop=(c == 31))
                    if c == 15 or c == 31:
                        mm.then_inc(pWH, 1)
                if kind == 'V':
                    pendV[0] = ur
        if pendV[0] is not None:
            emit_vt(pendV[0])
            pendV[0] = None

        t.wait_ge(dRP, NR)
        t.wait_ge(aVT, 4 * NV)
        t.wait_ge(s_ld, 16 * NCONST)

        def emit_lav(h, items, k):
            sp, cid, bnd = items[k]
            first = (k == 0) or (items[k - 1][0] != sp)
            off = 128 * bnd if (bnd is not None and bnd > 0) else 0
            ko = 20 * h + k
            g_ = 2 * h + sp
            t.wait_ge(aE, ko + 1)
            po = (g_ % 2) * 512
            if first:
                W(t, dREC, g_ - 1)
            t.matmul(P3[:, po + off:po + 512], ones128[:],
                     psm((k % 2) * 1024 + off, 512 - off),
                     start=first, stop=(bnd == 3))
            if first:
                W(t, dNR, g_ - 1)
            mm = t.matmul(P2[:, po + off:po + 512],
                          VSB[:, cid * 1024 + (h // G) * 128:
                              cid * 1024 + (h // G) * 128 + 128],
                          psm((k % 2) * 1024 + off, 512 - off),
                          start=first, stop=(bnd == 3))
            mm.then_inc(pAV, 1)

        idx_t = t.value_load(sel_sb[0:1, 0:1])
        for j in nc.Switch(engines=[t], index=[idx_t], n=2):
            items = csched(j)
            for h in range(H):
                kv = h // G
                for k, (sp, cid, bnd) in enumerate(items):
                    ko = 20 * h + k
                    off = 128 * bnd if (bnd is not None and bnd > 0) else 0
                    W(t, aE, ko - 1)
                    mm = t.matmul(P1[:, (k % 2) * 512 + off:(k % 2) * 512 + 512],
                                  KT[:, kv * T + cid * 128: kv * T + cid * 128 + 128],
                                  qa(h, sp * 512 + off, 512 - off),
                                  start=True, stop=(bnd is None))
                    if bnd is None:
                        mm.then_inc(pSc, 1)
                    else:
                        t.matmul(P1[:, (k % 2) * 512 + off:(k % 2) * 512 + 512],
                                 alw_sb[:],
                                 bsh_sb[:, 384 - 128 * bnd + off:
                                        384 - 128 * bnd + 512],
                                 start=False, stop=True).then_inc(pSc, 1)
                    if k >= 1:
                        emit_lav(h, items, k - 1)
                emit_lav(h, items, NCHUNK - 1)

        def d_sums(m):
            t.wait_ge(aCP, m + 1)
            for nt in range(2):
                t.matmul(P3[0:1, nt * 512: nt * 512 + 512], ones_bf[:],
                         SLB[:, (m % 2) * 1024 + nt * 512:
                             (m % 2) * 1024 + nt * 512 + 512],
                         start=(m == 0), stop=(m == 31))
            for nt in range(2):
                mm = t.matmul(P3[32:33, nt * 512: nt * 512 + 512], ones_bf[:],
                              SLB[:, 2048 + (m % 2) * 1024 + nt * 512:
                                  2048 + (m % 2) * 1024 + nt * 512 + 512],
                              start=False, stop=(m == 31),
                              skip_group_check=True)
                if nt == 1:
                    mm.then_inc(pS2, 1)

        t.wait_ge(dNR, 64)
        for m in range(32):
            W(t, dX2, m - 1)
            dpo = P2 if m % 2 == 0 else P1
            for c in range(NC32):
                if c == 0:
                    t.wait_ge(g_Wo0, 16 * (m + 1))
                if c == 16:
                    t.wait_ge(g_Wo1, 16 * (m + 1))
                for t_i in range(2):
                    mm = t.matmul(dpo[:, t_i * 512:(t_i + 1) * 512],
                                  WB[:, (2 * m + c // 16) % 2 * 2048
                                     + (c % 16) * 128:
                                     (2 * m + c // 16) % 2 * 2048
                                     + (c % 16) * 128 + 128],
                                  qa(c, t_i * 512, 512),
                                  start=(c == 0), stop=(c == 31))
                    if t_i == 1 and (c == 15 or c == 31):
                        mm.then_inc(pD, 1)
            if m >= 1:
                d_sums(m - 1)
        d_sums(31)
        t.wait_ge(dRS2, 1)
        t.wait_ge(dX2, 32)
        for nt in range(2):
            t.matmul(P2[:, nt * 512:(nt + 1) * 512], srf(ONES1, 128),
                     SRF[0:1, nt * 512: nt * 512 + 512], start=True, stop=True)
        for nt in range(2):
            mm = t.matmul(P3[:, nt * 512:(nt + 1) * 512], srf(ONES1, 128),
                          SRF[0:1, 2048 + nt * 512: 2048 + nt * 512 + 512],
                          start=True, stop=True)
            if nt == 1:
                mm.then_inc(pBC2, 1)
        for k2 in range(2):
            for t_i in range(2):
                u = 2 * k2 + t_i
                W(t, dXA, u - 1)
                for c in range(NC32):
                    if c == 0:
                        t.wait_ge(g_A, 32 * (k2 + 1))
                    t.wait_ge(dAPE, c + 1)
                    mm = t.matmul(P1[:, (u % 2) * 512:(u % 2) * 512 + 512],
                                  WB[:, (c // 16) * 2048 + (c % 16) * 128:
                                     (c // 16) * 2048 + (c % 16) * 128 + 128],
                                  qa(c, t_i * 512, 512),
                                  start=(c == 0), stop=(c == 31))
                    if c == 31:
                        mm.then_inc(pXA, 1)
        for m in range(32):
            if m == 0:
                t.wait_ge(dXA, 4)
            W(t, aF, m)
            for c in range(NC32):
                if c == 0:
                    t.wait_ge(g_Wf0, 16 * (m + 1))
                if c == 16:
                    t.wait_ge(g_Wf1, 16 * (m + 1))
                for t_i in range(2):
                    mm = t.matmul(P2[:, t_i * 512:(t_i + 1) * 512],
                                  WB[:, (2 * m + c // 16) % 2 * 2048
                                     + (c % 16) * 128:
                                     (2 * m + c // 16) % 2 * 2048
                                     + (c % 16) * 128 + 128],
                                  qa(c, t_i * 512, 512),
                                  start=(c == 0), stop=False)
                    if c == 15 and t_i == 1:
                        mm.then_inc(pF, 1)
            t.wait_ge(g_BT0 if m % 2 == 0 else g_BT1, 16 * (m // 2 + 1))
            for t_i in range(2):
                for k2 in range(2):
                    mm = t.matmul(P2[:, t_i * 512:(t_i + 1) * 512],
                                  BT[:, (m % 2) * 256 + k2 * 128:
                                     (m % 2) * 256 + k2 * 128 + 128],
                                  SLB[:, k2 * 1024 + t_i * 512:
                                      k2 * 1024 + t_i * 512 + 512],
                                  start=False, stop=(k2 == 1))
                    if t_i == 1 and k2 == 1:
                        mm.then_inc(pF, 1)

    @block.scalar
    def _(sc_):
        sc_.wait_ge(s_ld, 16 * NCONST)
        sc_.activation(srf(INVF, 64), i64_sb[:], AF.Exp, bias=zf,
                       scale=np.float32(-np.log(10000.0) / 64.0)).then_inc(aTB, 1)
        sc_.wait_ge(pW, 2)
        sc_.activation(wexp[:, 0:1], P3[:, 0:1], AF.Identity, bias=zf,
                       scale=onef)
        sc_.activation(wexp[:, 1:2], P3[:, 1:2], AF.Identity, bias=zf,
                       scale=onef).then_inc(aW, 2)
        for s_i in range(4):
            sc_.wait_ge(pTB, s_i + 1)
            W(sc_, dRP, SLAB_R0[s_i])
            sc_.activation(PFG[0:64, 2048:2560], P3[0:64, 512:1024],
                           AF.Identity, bias=zf, scale=onef).then_inc(aTB, 1)
            for c in range(NC32):
                u = 32 * s_i + c
                W(sc_, pS, u - 1)
                if c % 16 == 0:
                    sc_.wait_ge(g_X0 if c < 16 else g_X1,
                                256 * (s_i + 1))
                sc_.square(psm((c % 2) * 1024, 512),
                           slb(c, 0, 512)).then_inc(aSQ, 1)
            sc_.wait_ge(pS, 32 * (s_i + 1))
            sc_.activation(srf(MEAN, 512), P2[0:1, 0:512], AF.Identity,
                           bias=zf, scale=np.float32(1.0 / 4096.0))
            sc_.activation(srf(SSD, 512), P3[0:1, 0:512], AF.Identity,
                           bias=zf, scale=np.float32(1.0 / 4096.0))
            sc_.drain()
            sc_.square(srf(VTMP, 512), srf(MEAN, 512)).then_inc(aST, 1)
            sc_.wait_ge(dST, s_i + 1)
            sc_.activation(srf(SSD, 512), srf(SVAR, 512), AF.Sqrt,
                           bias=np.float32(EPS), scale=onef).then_inc(aST, 1)
            for (_s, kind, m, uG, ur) in SCHED_BY_SLAB[s_i]:
                sc_.wait_ge(pWH, 2 * (uG + 1))
                psrc = P1[:, (uG % 2) * 512:(uG % 2) * 512 + 512]
                if kind == 'V':
                    W(sc_, pVT, 4 * ur - 4)
                    sc_.activation(psm(512 + (ur % 2) * 1024, 512), psrc,
                                   AF.Identity, bias=zf,
                                   scale=onef).then_inc(aVE, 1)
                    for tc in range(4):
                        vt = 4 * ur + tc
                        sc_.wait_ge(pVT, vt + 1)
                        sc_.activation(VSB[:, (4 * s_i + tc) * 1024 + m * 128:
                                           (4 * s_i + tc) * 1024 + m * 128 + 128],
                                       P5[:, (vt % 2) * 1024:(vt % 2) * 1024 + 128],
                                       AF.Identity, bias=zf,
                                       scale=onef).then_inc(aVT, 1)
                else:
                    sc_.wait_ge(dTB, s_i + 1)
                    W(sc_, dRP, ur - 1)
                    sc_.activation(pfg(2048 + (ur % 2) * 512, 512), psrc,
                                   AF.Identity, bias=zf,
                                   scale=onef).then_inc(aKE, 1)
        idx_s = sc_.value_load(sel_sb[0:1, 0:1])
        for j in nc.Switch(engines=[sc_], index=[idx_s], n=2):
            items = csched(j)
            for h in range(H):
                for k, (sp, cid, bnd) in enumerate(items):
                    ko = 20 * h + k
                    off = 128 * bnd if (bnd is not None and bnd > 0) else 0
                    sc_.wait_ge(pSc, ko + 1)
                    W(sc_, pAV, ko - 1)
                    sc_.activation(psm((k % 2) * 1024 + off, 512 - off),
                                   P1[:, (k % 2) * 512 + off:
                                      (k % 2) * 512 + 512],
                                   AF.Exp, bias=np.float32(EXP_B),
                                   scale=np.float32(SCL)).then_inc(aE, 1)
        for m in range(32):
            sc_.wait_ge(dX2, m + 1)
            W(sc_, pS2, m - 1)
            sc_.activation(SLB[:, (m % 2) * 1024: (m % 2) * 1024 + 1024],
                           pfg((m % 2) * 1024, 1024),
                           AF.Identity, bias=zf, scale=onef)
            sc_.square(SLB[:, 2048 + (m % 2) * 1024: 2048 + (m % 2) * 1024 + 1024],
                       pfg((m % 2) * 1024, 1024)).then_inc(aCP, 1)
        sc_.wait_ge(pS2, 32)
        sc_.activation(SRF[0:1, 0:512], P3[0:1, 0:512], AF.Identity,
                       bias=zf, scale=np.float32(1.0 / 4096.0))
        sc_.activation(SRF[0:1, 512:1024], P3[0:1, 512:1024], AF.Identity,
                       bias=zf, scale=np.float32(1.0 / 4096.0))
        sc_.activation(SRF[0:1, 2048:2560], P3[32:33, 0:512], AF.Identity,
                       bias=zf, scale=np.float32(1.0 / 4096.0))
        sc_.activation(SRF[0:1, 2560:3072], P3[32:33, 512:1024], AF.Identity,
                       bias=zf, scale=np.float32(1.0 / 4096.0))
        sc_.drain()
        sc_.square(SRF[0:1, 1024:1536], SRF[0:1, 0:512])
        sc_.square(SRF[0:1, 1536:2048], SRF[0:1, 512:1024]).then_inc(aST2, 1)
        sc_.wait_ge(dST2, 1)
        sc_.activation(SRF[0:1, 1024:1536], SRF[0:1, 2048:2560], AF.Sqrt,
                       bias=np.float32(EPS), scale=onef)
        sc_.activation(SRF[0:1, 1536:2048], SRF[0:1, 2560:3072], AF.Sqrt,
                       bias=np.float32(EPS), scale=onef).then_inc(aST2, 1)
        for m in range(32):
            sc_.wait_ge(pF, 2 * (m + 1))
            W(sc_, s_OUT0 if m % 2 == 0 else s_OUT1, 16 * (m // 2))
            sc_.activation(pfg((m % 2) * 1024, 1024), P2[:, 0:1024],
                           AF.Identity, bias=bias_sb[:, m:m + 1],
                           scale=onef).then_inc(aF, 1)

    class _DrainVec:
        _OPS = {"memset", "tensor_copy", "tensor_tensor", "tensor_add",
                "tensor_sub", "tensor_mul", "tensor_max", "tensor_scalar",
                "tensor_single_scalar", "tensor_scalar_add",
                "tensor_scalar_sub", "tensor_scalar_mul", "tensor_scalar_max",
                "tensor_scalar_min", "scalar_tensor_tensor", "reciprocal"}

        def __init__(self, eng):
            self._e = eng

        def __getattr__(self, n):
            a = getattr(self._e, n)
            if n in _DrainVec._OPS:
                def wrap(*args, **kw):
                    inst = a(*args, **kw)
                    self._e.drain()
                    return inst
                return wrap
            return a

    @block.vector
    def _(v_raw):
        v = _DrainVec(v_raw)
        v.memset(ones_bf[:], 1.0)
        v.memset(ones128[:], 1.0)
        v.memset(srf(ONES1, 128), 1.0)
        v.wait_ge(s_ld, 16 * NCONST)
        v.tensor_mul(wvec[:], cf_sb[:], sc_sb[:]).then_inc(dCN, 3)
        for s_i in range(4):
            v.wait_ge(s_TB, 32 * s_i)
            ang = PFG[0:64, 2048:2560]
            angr = PFG[0:64, 2560:3072]
            u_t = PFG[0:64, 3072:3584]
            q_t = PFG[0:64, 3584:4096]
            sinp = PFG[0:64, 1024:1536]
            v.wait_ge(aTB, s_i + 2)
            v.tensor_scalar(u_t, ang, INV2PI, MAGIC, ALU.mult, ALU.add)
            v.tensor_scalar_add(u_t, u_t, -MAGIC)
            v.scalar_tensor_tensor(angr, u_t, NEG2PI, ang, ALU.mult, ALU.add)
            v.tensor_mul(u_t, angr, angr)
            v.tensor_scalar_mul(ang, angr, SIN_LEAD)
            v.tensor_scalar_add(q_t, u_t, SIN_D[4])
            for k in (3, 2, 1):
                v.scalar_tensor_tensor(q_t, q_t, SIN_D[k], u_t, ALU.add, ALU.mult)
            v.scalar_tensor_tensor(sinp, q_t, SIN_D[0], ang, ALU.add, ALU.mult)
            v.tensor_scalar_mul(PFG[0:64, 512:1024], sinp, -1.0)
            v.tensor_scalar_add(q_t, u_t, COS_D[5])
            for k in (4, 3, 2, 1):
                v.scalar_tensor_tensor(q_t, q_t, COS_D[k], u_t, ALU.add, ALU.mult)
            v.tensor_scalar(PFG[0:64, 0:512], q_t, COS_D[0], COS_LEAD,
                            ALU.add, ALU.mult).then_inc(dTB, 1)
            v.wait_ge(aST, 2 * s_i + 1)
            v.tensor_sub(srf(SVAR, 512), srf(SSD, 512),
                         srf(VTMP, 512)).then_inc(dST, 1)
            v.wait_ge(aST, 2 * s_i + 2)
            v.reciprocal(srf(RSTD, 512), srf(SSD, 512)).then_inc(dRS, 1)
            v.wait_ge(pBC, s_i + 1)
            v.wait_ge(s_TB, 32 * (s_i + 1))
            for cp in range(16):
                c = 2 * cp
                t1 = pfg(1024, 1024)
                v.tensor_sub(t1, slb(c, 0, 1024), P2[:, 0:1024])
                v.tensor_mul(slb(c, 0, 1024), t1,
                             P3[:, 0:1024]).then_inc(dAP, 2)
            v.wait_ge(s_TB, 32 * (s_i + 1))
            for (_s, kind, m, uG, ur) in SCHED_BY_SLAB[s_i]:
                if kind == 'V':
                    continue
                v.wait_ge(s_RH0 if ur % 2 == 0 else s_RH1, 32 * (ur // 2 + 1))
                stg = pfg(2048 + (ur % 2) * 512, 512)
                rh = pfg(3072 + (ur % 2) * 512, 512)
                t1 = pfg(1024, 512)
                t2 = pfg(1536, 512)
                v.tensor_mul(t1, stg, pfg(0, 512))
                v.tensor_mul(t2, rh, pfg(512, 512))
                if kind == 'K':
                    dstp = KT[:, m * T + s_i * 512: m * T + s_i * 512 + 512]
                else:
                    dstp = qa(m, s_i * 512, 512)
                v.tensor_add(dstp, t1, t2).then_inc(dRP, 1)
        idx_v = v_raw.value_load(sel_sb[0:1, 0:1])
        for j in nc.Switch(engines=[v_raw], index=[idx_v], n=2):
            for h in range(H):
                for g_l, endk in span_ends(j):
                    g_ = 2 * h + g_l
                    po = (g_ % 2) * 512
                    v.wait_ge(pAV, 20 * h + endk + 1)
                    v.reciprocal(pfg(2048 + (g_ % 2) * 512, 512),
                                 P3[:, po:po + 512]).then_inc(dREC, 1)
                    v.tensor_mul(qa(h, g_l * 512, 512), P2[:, po:po + 512],
                                 pfg(2048 + (g_ % 2) * 512, 512)) \
                        .then_inc(dNR, 1)
        for m in range(32):
            v.wait_ge(pD, 2 * (m + 1))
            v.wait_ge(s_X0 if m % 2 == 0 else s_X1, 16 * (m // 2 + 1))
            W(v, s_SC0 if m % 2 == 0 else s_SC1, 16 * (m // 2))
            W(v, aCP, m - 1)
            v.tensor_add(pfg((m % 2) * 1024, 1024),
                         (P2 if m % 2 == 0 else P1)[:, 0:1024],
                         pfg(2048 + (m % 2) * 1024, 1024)).then_inc(dX2, 1)
        v.wait_ge(aST2, 1)
        v.tensor_sub(SRF[0:1, 2048:2560], SRF[0:1, 2048:2560],
                     SRF[0:1, 1024:1536])
        v.tensor_sub(SRF[0:1, 2560:3072], SRF[0:1, 2560:3072],
                     SRF[0:1, 1536:2048]).then_inc(dST2, 1)
        v.wait_ge(aST2, 2)
        v.reciprocal(SRF[0:1, 2048:2560], SRF[0:1, 1024:1536])
        v.reciprocal(SRF[0:1, 2560:3072], SRF[0:1, 1536:2048]).then_inc(dRS2, 1)
        v.wait_ge(pBC2, 1)
        v.wait_ge(pD, 64)
        v.wait_ge(s_SC0, 16 * 16)
        v.wait_ge(s_SC1, 16 * 16)
        for c in range(NC32):
            v.wait_ge(g_X20 if c % 2 == 0 else g_X21, 16 * (c // 2 + 1))
            t1 = pfg((c % 2) * 1024, 1024)
            v.tensor_sub(t1, SLB[:, 6144 + (c % 8) * 1024:
                                 6144 + (c % 8) * 1024 + 1024], P2[:, 0:1024])
            v.tensor_mul(qa(c, 0, OWN), t1, P3[:, 0:1024]).then_inc(dAPE, 1)
        for k2 in range(2):
            for t_i in range(2):
                u = 2 * k2 + t_i
                v.wait_ge(pXA, u + 1)
                v.tensor_scalar_mul(SLB[:, k2 * 1024 + t_i * 512:
                                        k2 * 1024 + t_i * 512 + 512],
                                    P1[:, (u % 2) * 512:(u % 2) * 512 + 512],
                                    wexp[:, k2:k2 + 1]).then_inc(dXA, 1)


    nc._kernel_ctx = ctx
    return nc


def _lhsT_tiles(Wdo):
    Din, Dout = Wdo.shape
    return np.ascontiguousarray(
        Wdo.reshape(Din // 128, 128, Dout // 128, 128).transpose(2, 1, 0, 3))


def _consts():
    alow = np.zeros((128, 128), np.float32)
    for m_ in range(128):
        alow[m_, m_:] = -1e30
    bshv = np.zeros((128, 896), np.float32)
    for zc in range(896):
        z = zc - 384
        r = z + 1
        if r < 0:
            bshv[0, zc] = 1.0
        elif r < 128:
            bshv[r, zc] = 1.0
    e16v = np.zeros((2, 16, 128), np.float32)
    for k2 in range(2):
        for p_ in range(128):
            e16v[k2, (128 * k2 + p_) >> 4, p_] = 1.0
    idnv = np.eye(128, dtype=np.float32)
    return (alow.astype(ml_dtypes.bfloat16), bshv.astype(ml_dtypes.bfloat16),
            e16v, idnv.astype(ml_dtypes.bfloat16))


def kernel(L0_attn_X, L0_attn_Wq, L0_attn_Wk, L0_attn_Wv, L0_attn_Wo,
           L0_ffn_W, L0_ffn_bias, L0_ffn_A, L0_ffn_Bf, L0_ffn_coeffs,
           L0_ffn_scales):
    X = np.asarray(L0_attn_X, np.float32)
    wq_t = _lhsT_tiles(np.asarray(L0_attn_Wq, np.float32))
    wk_t = _lhsT_tiles(np.asarray(L0_attn_Wk, np.float32))
    wv_t = _lhsT_tiles(np.asarray(L0_attn_Wv, np.float32))
    wo_t = _lhsT_tiles(np.asarray(L0_attn_Wo, np.float32))
    wf_t = _lhsT_tiles(np.ascontiguousarray(np.asarray(L0_ffn_W, np.float32).T))
    A2 = np.asarray(L0_ffn_A, np.float32).reshape(256, 4096)
    wa_t = _lhsT_tiles(np.ascontiguousarray(A2.T))
    Bcat = np.ascontiguousarray(
        np.transpose(np.asarray(L0_ffn_Bf, np.float32), (0, 2, 1)).reshape(256, 4096))
    wb_t = _lhsT_tiles(Bcat)
    bias_t = np.ascontiguousarray(
        np.asarray(L0_ffn_bias, np.float32).reshape(32, 128).T)
    cf_v = np.asarray(L0_ffn_coeffs, np.float32).reshape(16, 1)
    sc_v = np.asarray(L0_ffn_scales, np.float32).reshape(16, 1)
    alow, bshv, e16v, idnv = _consts()
    i64_v = np.arange(64, dtype=np.float32).reshape(1, 64)

    shared = dict(wq=wq_t, wk=wk_t, wv=wv_t, wo=wo_t, wf=wf_t, wa=wa_t,
                  wb=wb_t, bi=bias_t, cf=cf_v, scl=sc_v, e16=e16v,
                  alw=alow, bsh=bshv, idn=idnv, i64=i64_v)

    in_maps = []
    spans = []
    for core in range(8):
        b_, j_ = divmod(core, 2)
        a0 = 512 * j_
        b0 = 1536 - 512 * j_
        own = list(range(a0, a0 + 512)) + list(range(b0, b0 + 512))
        other = [p for p in range(T) if not (a0 <= p < a0 + 512
                                             or b0 <= p < b0 + 512)]
        perm = np.array(own + other, dtype=np.int64)
        spans.append((b_, a0, b0))
        XT = np.ascontiguousarray(X[b_].T[:, perm])
        iov = perm.astype(np.float32).reshape(1, T)
        m = dict(shared)
        m["xf"] = XT
        m["io"] = iov
        m["sel"] = np.array([[j_]], np.int32)
        in_maps.append(m)

    nc = build_program()
    res = run_bass_kernel_spmd(nc, in_maps, core_ids=list(range(8)))

    outp = np.empty((B, T, D), np.float32)
    for core in range(8):
        b_, a0, b0 = spans[core]
        o = res.results[core]["out"]
        outp[b_, a0:a0 + 512, :] = o[:, 0:512].T
        outp[b_, b0:b0 + 512, :] = o[:, 512:1024].T
    return outp
